# version 02 (final)
# baseline (speedup 1.0000x reference)
import numpy as np
import ml_dtypes
from contextlib import ExitStack

import concourse.bass as bass
import concourse.mybir as mybir
from concourse.bass_utils import run_bass_kernel_spmd

F32 = mybir.dt.float32
BF16 = mybir.dt.bfloat16
I32 = mybir.dt.int32
AF = mybir.ActivationFunctionType
ALU = mybir.AluOpType
AX = mybir.AxisListType

NF4 = [-1.0, -0.6961928009986877, -0.5250730514526367, -0.39491748809814453,
       -0.28444138169288635, -0.18477343022823334, -0.09105003625154495, 0.0,
       0.07958029955625534, 0.16093020141124725, 0.2461123019456863,
       0.33791524171829224, 0.44070982933044434, 0.5626170039176941,
       0.7229568362236023, 1.0]

NCORES = 8
DM = 4096
DFF = 11008
NOT = DFF // 128
R = 16
SEQ = 8192
SLOC = 1024
HD = 128
ESL = 512
LN_EPS = 1e-5
CLIP = 8.0
SCALE = float(HD) ** -0.5
NEG_BIG = -1e30

GU_CHUNKS = [(0, 8), (8, 28), (28, 57), (57, 86)]
SERIAL = 1


def ts(i, w):
    return slice(i * w, (i + 1) * w)


def sl(a, w):
    return slice(a, a + w)


class Plan:
    ENGS = ("sync", "tensor", "vector", "scalar", "gpsimd")

    def __init__(self, ctx, nc):
        self.nc = nc
        self.E = {e: [] for e in self.ENGS}
        self.ctx = ctx
        self.sems = {}

    def sem(self, name):
        h = self.ctx.enter_context(self.nc.semaphore(name))
        self.sems[name] = [h, 0]

    def cnt(self, name):
        return self.sems[name][1]

    def emit(self, eng, fn):
        self.E[eng].append(fn)

    def wait(self, eng, name, val):
        if val <= 0:
            return
        h = self.sems[name][0]
        self.emit(eng, lambda e: e.wait_ge(h, val))

    def op(self, eng, fn, inc=None, k=1):
        if inc is None:
            self.emit(eng, fn)
            return None
        h = self.sems[inc][0]
        self.emit(eng, lambda e: fn(e).then_inc(h, k))
        self.sems[inc][1] += k
        return self.sems[inc][1]

    def fence(self):
        for eng in self.ENGS:
            for nm, (h, cnt) in self.sems.items():
                if cnt > 0:
                    self.emit(eng, lambda e, h=h, c=cnt: e.wait_ge(h, c))


def build_program(stage=4, dbg=()):
    nc = bass.Bass("TRN2", target_bir_lowering=False, num_devices=NCORES)

    x_in = nc.dram_tensor("x", [SLOC, DM], F32, kind="ExternalInput").ap()
    gcodes = nc.dram_tensor("gcodes", [DFF, 512], I32, kind="ExternalInput").ap()
    gabs = nc.dram_tensor("gabs", [DFF, 8], F32, kind="ExternalInput").ap()
    ucodes = nc.dram_tensor("ucodes", [DFF, 512], I32, kind="ExternalInput").ap()
    uabs = nc.dram_tensor("uabs", [DFF, 8], F32, kind="ExternalInput").ap()
    dcodes = nc.dram_tensor("dcodes", [DM, 1376], I32, kind="ExternalInput").ap()
    dabsx = nc.dram_tensor("dabsx", [DM, 1376], F32, kind="ExternalInput").ap()
    glA = nc.dram_tensor("glA", [DM, R], F32, kind="ExternalInput").ap()
    glB = nc.dram_tensor("glB", [R, DFF], F32, kind="ExternalInput").ap()
    ulA = nc.dram_tensor("ulA", [DM, R], F32, kind="ExternalInput").ap()
    ulB = nc.dram_tensor("ulB", [R, DFF], F32, kind="ExternalInput").ap()
    dlA = nc.dram_tensor("dlA", [DFF, R], F32, kind="ExternalInput").ap()
    dlB = nc.dram_tensor("dlB", [R, DM], F32, kind="ExternalInput").ap()
    wqkv_sl = nc.dram_tensor("wqkv_sl", [1536, DM], F32, kind="ExternalInput").ap()
    wout_sl = nc.dram_tensor("wout_sl", [DM, 512], F32, kind="ExternalInput").ap()
    qlnw = nc.dram_tensor("qlnw", [ESL], F32, kind="ExternalInput").ap()
    qlnb = nc.dram_tensor("qlnb", [ESL], F32, kind="ExternalInput").ap()
    klnw = nc.dram_tensor("klnw", [ESL], F32, kind="ExternalInput").ap()
    klnb = nc.dram_tensor("klnb", [ESL], F32, kind="ExternalInput").ap()
    slopes_in = nc.dram_tensor("slopes", [1, 4], F32, kind="ExternalInput").ap()
    ident_in = nc.dram_tensor("ident", [128, 128], BF16, kind="ExternalInput").ap()
    identf_in = nc.dram_tensor("identf", [128, 128], F32, kind="ExternalInput").ap()

    out_ext = nc.dram_tensor("out", [SLOC, DM], F32, kind="ExternalOutput").ap()
    dbg_aps = {}

    def dbg_tensor(name, shape, dt):
        if name in dbg:
            dbg_aps[name] = nc.dram_tensor("dbg_" + name, shape, dt,
                                           kind="ExternalOutput").ap()
        return dbg_aps.get(name)

    hb = {}
    for nm in ("g", "u"):
        for k, (o0, o1) in enumerate(GU_CHUNKS):
            n = o1 - o0
            hb[f"{nm}{k}_in"] = nc.dram_tensor(f"{nm}{k}_in", [n, 512, 128], BF16).ap()
            hb[f"{nm}{k}_out"] = nc.dram_tensor(
                f"{nm}{k}_out", [NCORES, n, 512, 128], BF16).ap()
    for k in range(2):
        hb[f"d{k}_in"] = nc.dram_tensor(f"d{k}_in", [1376, 2048], BF16).ap()
        hb[f"d{k}_out"] = nc.dram_tensor(f"d{k}_out", [NCORES * 1376, 2048], BF16).ap()
    wqkvT_loc = nc.dram_tensor("wqkvT_loc", [DM, 1536], BF16).ap()
    hb["wout_in"] = nc.dram_tensor("wout_in", [512, DM], BF16).ap()
    hb["wout_out"] = nc.dram_tensor("wout_out", [DM, DM], BF16).ap()
    xres1 = nc.dram_tensor("xres1", [SLOC, DM], F32).ap()
    xag_in = nc.dram_tensor("xag_in", [SLOC, DM], BF16).ap()
    xag_out = [nc.dram_tensor(f"xag_out{h}", [NCORES, 512, DM], BF16).ap()
               for h in range(2)]
    qk_stash = nc.dram_tensor("qk_stash", [SEQ, 1024], F32).ap()
    vhat = nc.dram_tensor("vhat", [SEQ, ESL], BF16).ap()
    stats_cc = nc.dram_tensor("stats_cc", [128, 64 * 4], F32).ap()
    qhat = nc.dram_tensor("qhat", [SEQ, ESL], BF16).ap()
    khat = nc.dram_tensor("khat", [SEQ, ESL], BF16).ap()
    ot_in = [nc.dram_tensor(f"ot_in{k}", [ESL, 2048], BF16).ap() for k in range(4)]
    rs_in = [nc.dram_tensor(f"rs_in{q}", [SEQ, 1024], F32).ap() for q in range(4)]
    rs_out = [nc.dram_tensor(f"rs_out{q}", [SLOC, 1024], F32).ap()
              for q in range(4)]
    xres2 = nc.dram_tensor("xres2", [SLOC, DM], F32).ap()

    ctx = ExitStack()
    with ctx:
        sbuf = lambda n, sh, dt: ctx.enter_context(nc.sbuf_tensor(n, sh, dt))
        psum = lambda n, sh, dt: ctx.enter_context(nc.psum_tensor(n, sh, dt))
        P = Plan(ctx, nc)
        for s in ("sLd", "sGp", "sWr", "sCC", "sPE", "sVe", "sSc",
                  "sGw", "sXc", "sSt", "sGs"):
            P.sem(s)

        s_id = sbuf("s_id", [128, 128], BF16)
        s_idf = sbuf("s_idf", [128, 128], F32)
        s_gA = sbuf("s_gA", [128, 32 * R], BF16)
        s_uA = sbuf("s_uA", [128, 32 * R], BF16)
        s_dA = sbuf("s_dA", [128, NOT * R], BF16)
        s_lB = sbuf("s_lB", [80, DFF], BF16)
        s_gB = s_lB[0:16, :]
        s_uB = s_lB[32:48, :]
        s_dB = s_lB[64:80, 0:DM]
        NB = 2
        s_tbig = sbuf("s_tbig", [80, 512], BF16)
        s_tg = s_tbig[0:16, :]
        s_tu = s_tbig[32:48, :]
        s_t2 = s_tbig[64:80, :]
        rem = nc.sbuf_bytes_remaining
        uni_cols = (rem - 1536) // 2
        uni = sbuf("uni", [128, uni_cols], BF16)
        u32 = uni[:].bitcast(F32)

        def vb(off, n):
            return uni[:, sl(off, n)]

        def vf(off, n):
            return u32[:, sl(off, n)]
        s_x3T = vb(0, NOT * 512)
        s_x1T = vb(44032, 16384)
        s_gw = vb(60416, 4096)
        s_uw = vb(64512, 4096)
        s_wd = [vb(68608, 512), vb(69120, 512)]
        s_y1r = [vb(69632, 512), vb(70144, 512)]
        s_xpc = [vf(35328, 512), vf(35840, 512)]
        s_xrp = [vf(36352, 512), vf(36864, 512)]
        s_cb = [vb(74752, 512), vb(75264, 512)]
        s_term = vb(75776, 512)
        s_vp = vb(76288, 512)
        s_vn = vb(76800, 512)
        s_ab = [vf(38656, 8), vf(38664, 8)]
        s_abx = [vf(38672, 512), vf(39184, 512)]
        s_wde = [vb(79392, 512), vb(79904, 512)]
        s_wtr = [vb(80416, 512), vb(80928, 512)]
        v_xc = [vf(0, DM), vf(DM, DM)]
        v_scr = vf(2 * DM, DM)
        v_xnb = vf(3 * DM, DM)
        v_xnbb = vb(8 * DM, DM)
        q_wqv = vb(0, 32 * 1536)
        q_x1aT = [vb(49152, 4096), vb(53248, 4096)]
        q_xagc = [vb(57344, 4096), vb(61440, 4096)]
        q_apf = vf(32768, DM)
        q_apb = vb(73728, 4096)
        q_qkvc = [vf(38912, 1536), vf(40448, 1536)]
        q_bc = [vf(41984 + i * 512, 512) for i in range(4)]
        q_stat = vf(44032, 256)
        q_statR = vf(28672, 256)
        q_mu = vf(28928, 64 * 6)
        q_kld = [vf(24576, 1024), vf(25600, 1024)]
        q_cvf = q_apf
        q_cvb = q_apb
        q_cvst = q_xagc[0]
        s_sm = sbuf("s_sm", [128, 1], F32)
        s_ss = sbuf("s_ss", [128, 1], F32)
        s_mu = sbuf("s_mu", [128, 1], F32)
        s_var = sbuf("s_var", [128, 1], F32)
        s_rstd = sbuf("s_rstd", [128, 1], F32)
        s_nmr = sbuf("s_nmr", [128, 1], F32)
        s_nmu2 = sbuf("s_nmu2", [128, 1], F32)
        s_eps = sbuf("s_eps", [128, 1], F32)
        s_slope = sbuf("s_slope", [128, 4], F32)
        s_negmax = sbuf("s_negmax", [128, 1], F32)
        s_warm = sbuf("s_warm", [128, 1], F32)
        s_warm2 = sbuf("s_warm2", [128, 1], F32)

        pb = [psum(f"pb{i}", [128, 512], F32) for i in range(4)]
        pbh = [psum(f"pbh{i}", [128, 1024], BF16) for i in range(2)]
        pt = [psum(f"pt{i}", [128, 1024], BF16) for i in range(2)]

        ve0 = P.op("vector", lambda e: e.memset(s_eps[:], LN_EPS), "sVe")
        P.wait("scalar", "sVe", ve0)
        for f in (AF.Copy, AF.Square, AF.Relu, AF.Exp, AF.Ln):
            P.op("scalar", lambda e, f=f: e.activation(
                out=s_warm[:], in_=s_eps[:], func=f), None)
        P.op("scalar", lambda e: e.activation(
            out=s_warm[:], in_=s_eps[:], func=AF.Square, bias=s_eps[:],
            accum_out=s_warm2[:]), None)
        P.op("scalar", lambda e: e.copy(s_warm[:], s_eps[:]), None)
        P.op("sync", lambda e: e.dma_start(out=s_id[:], in_=ident_in), "sLd", 16)
        P.op("sync", lambda e: e.dma_start(
            out=s_slope[:], in_=slopes_in.broadcast_to([128, 4])), "sLd", 16)
        P.op("sync", lambda e: e.dma_start(out=s_idf[:], in_=identf_in), "sLd", 16)
        for dst, src in ((s_gA, glA), (s_uA, ulA), (s_dA, dlA)):
            P.op("gpsimd", lambda e, d=dst, s=src: e.dma_start(
                out=d[:].rearrange("p (t r) -> p t r", r=R),
                in_=s.rearrange("(t p) r -> p t r", p=128)), "sGp", 16)
        for dst, srcap in ((s_gB, glB), (s_uB, ulB), (s_dB, dlB)):
            P.op("gpsimd", lambda e, d=dst, s=srcap: e.dma_start(out=d, in_=s),
                 "sGp", 16)
        lora_gp = P.cnt("sGp")
        id_ld = P.cnt("sLd")

        raw_jobs = []
        for nm, codes_ap, abs_ap in (("g", gcodes, gabs), ("u", ucodes, uabs)):
            for k, (o0, o1) in enumerate(GU_CHUNKS):
                for otl, ot in enumerate(range(o0, o1)):
                    raw_jobs.append(dict(kind=nm, k=k, otl=otl,
                                         codes=codes_ap[ts(ot, 128), :],
                                         absm=abs_ap[ts(ot, 128), :], ncols=512))
        for dmt in range(32):
            for f0, f1 in ((0, 512), (512, 1024), (1024, 1376)):
                raw_jobs.append(dict(kind="d", dmt=dmt, f0=f0,
                                     codes=dcodes[ts(dmt, 128), f0:f1],
                                     absm=dabsx[ts(dmt, 128), f0:f1],
                                     ncols=f1 - f0))
        gi = lambda k: list(range(sum(b - a for a, b in GU_CHUNKS[:k]),
                                  sum(b - a for a, b in GU_CHUNKS[:k + 1])))
        ui = lambda k: [86 + i for i in gi(k)]
        dj = list(range(172, 268))
        d0_jobs = [j for j in dj if raw_jobs[j]["dmt"] < 16]
        d1_jobs = [j for j in dj if raw_jobs[j]["dmt"] >= 16]
        order = (gi(0) + ui(0) + gi(1) + ui(1) + gi(2) + ui(2) + d0_jobs
                 + gi(3) + ui(3) + d1_jobs)
        order = [0] + order
        import copy as _copy
        dq_jobs = [raw_jobs[i] for i in order]
        dq_jobs[0] = dict(raw_jobs[0])
        ag_bounds = [("g", 1, 57), ("u", 1, 77), ("g", 2, 106), ("u", 2, 135),
                     ("d", 0, 183), ("g", 3, 212), ("u", 3, 241), ("d", 1, 269)]

        dqs = dict(j=0, lut=[0, 0], deq=[0, 0], tr=[0, 0], ev=[0, 0], wr=[0, 0])
        ag_cc = {}

        def plan_ag(key, wr_cnt, extra=None):
            src, dst = hb[f"{key}_in"], hb[f"{key}_out"]
            P.wait("gpsimd", "sWr", wr_cnt)
            if extra:
                P.wait("gpsimd", extra[0], extra[1])
            P.op("gpsimd", lambda e, src=src, dst=dst: e.collective_compute(
                "AllGather", ALU.bypass, replica_groups=[list(range(NCORES))],
                ins=[src.flatten().unsqueeze(0)],
                outs=[dst.flatten().unsqueeze(0)]), "sCC")
            return P.cnt("sCC")

        def plan_dq_one():
            j = dqs["j"]
            if j >= len(dq_jobs):
                return False
            job = dq_jobs[j]
            b = j % NB
            ncc = job["ncols"]
            nt = (ncc + 127) // 128
            if j >= NB:
                P.wait("gpsimd", "sVe", dqs["deq"][b])
                P.wait("sync", "sVe", dqs["deq"][b])
            P.op("gpsimd", lambda e, job=job, b=b, ncc=ncc: e.dma_start(
                out=s_cb[b][:, 0:ncc], in_=job["codes"]), "sGp", 16)
            gp_after = P.cnt("sGp")
            if job["kind"] == "d":
                P.op("sync", lambda e, job=job, b=b, ncc=ncc: e.dma_start(
                    out=s_abx[b][:, 0:ncc], in_=job["absm"]), "sLd", 16)
            else:
                P.op("sync", lambda e, job=job, b=b: e.dma_start(
                    out=s_ab[b][:], in_=job["absm"]), "sLd", 16)
            ld_after = P.cnt("sLd")
            P.wait("vector", "sGp", gp_after)
            P.wait("vector", "sLd", ld_after)
            if j >= 2:
                P.wait("vector", "sPE", dqs["tr"][b])
            cb = s_cb[b][:, 0:ncc]

            def lut(e, cb=cb, ncc=ncc):
                e.tensor_scalar(s_vp[:, 0:ncc], cb, 7.5, NF4[8], ALU.is_ge, ALU.mult)
                for t in range(9, 16):
                    e.tensor_scalar(s_term[:, 0:ncc], cb, t - 0.5, NF4[t],
                                    ALU.is_ge, ALU.mult)
                    e.tensor_max(s_vp[:, 0:ncc], s_vp[:, 0:ncc], s_term[:, 0:ncc])
                e.tensor_scalar(s_vn[:, 0:ncc], cb, 0.5, NF4[0], ALU.is_le, ALU.mult)
                for t in range(1, 8):
                    e.tensor_scalar(s_term[:, 0:ncc], cb, t + 0.5, NF4[t],
                                    ALU.is_le, ALU.mult)
                    e.tensor_tensor(s_vn[:, 0:ncc], s_vn[:, 0:ncc],
                                    s_term[:, 0:ncc], ALU.min)
                return e.tensor_add(s_vn[:, 0:ncc], s_vn[:, 0:ncc], s_vp[:, 0:ncc])
            dqs["lut"][b] = P.op("vector", lut, "sVe")
            if job["kind"] == "d":
                def absm(e, b=b, ncc=ncc):
                    return e.tensor_mul(s_wde[b][:, 0:ncc], s_vn[:, 0:ncc],
                                        s_abx[b][:, 0:ncc])
            else:
                def absm(e, b=b):
                    for blk in range(8):
                        i = e.tensor_scalar(s_wde[b][:, ts(blk, 64)],
                                            s_vn[:, ts(blk, 64)],
                                            s_ab[b][:, blk:blk + 1], None, ALU.mult)
                    return i
            dqs["deq"][b] = P.op("vector", absm, "sVe")
            P.wait("tensor", "sVe", dqs["deq"][b])
            if j >= 2:
                P.wait("tensor", "sSc", dqs["ev"][j % 2])

            def trans(e, b=b, ncc=ncc, nt=nt, bank=j % 2):
                for t in range(nt):
                    w = min(128, ncc - t * 128)
                    i = e.transpose(pt[bank][0:w, ts(t, 128)],
                                    s_wde[b][:, t * 128:t * 128 + w], s_id[:])
                return i
            dqs["tr"][b] = P.op("tensor", trans, "sPE")
            P.wait("scalar", "sPE", dqs["tr"][b])
            if j >= NB:
                P.wait("scalar", "sWr", dqs["wr"][b])

            dqs["ev"][j % 2] = P.op(
                "scalar", lambda e, b=b, nt=nt, bank=j % 2: e.copy(
                    s_wtr[b][:, 0:nt * 128], pt[bank][:, 0:nt * 128]), "sSc")
            P.wait("sync", "sSc", dqs["ev"][j % 2])
            if job["kind"] == "d":
                dmt, f0 = job["dmt"], job["f0"]
                half = 0 if dmt < 16 else 1
                col0 = dmt * 128 - half * 2048

                def wr(e, b=b, ncc=ncc, nt=nt, half=half, col0=col0, f0=f0):
                    h = P.sems["sWr"][0]
                    for t in range(nt):
                        w = min(128, ncc - t * 128)
                        e.dma_start(
                            out=hb[f"d{half}_in"][f0 + t * 128:f0 + t * 128 + w,
                                                  col0:col0 + 128],
                            in_=s_wtr[b][0:w, ts(t, 128)]).then_inc(h, 16)
                P.emit("sync", wr)
                P.sems["sWr"][1] += 16 * nt
            else:
                dst = hb[f"{job['kind']}{job['k']}_in"][job["otl"]]
                P.op("sync", lambda e, b=b, dst=dst: e.dma_start(
                    out=dst.rearrange("(t p) o -> p t o", p=128),
                    in_=s_wtr[b][:].rearrange("p (t o) -> p t o", o=128)),
                    "sWr", 16)
            dqs["wr"][b] = P.cnt("sWr")
            job["wr_cnt"] = P.cnt("sWr")
            dqs["j"] = j + 1
            if SERIAL >= 2:
                P.fence()
            return True

        ag_q = list(ag_bounds)

        def plan_dq(n):
            for _ in range(n):
                if not plan_dq_one():
                    break
            while ag_q and dqs["j"] >= ag_q[0][2]:
                nm, k, bnd = ag_q.pop(0)
                ag_cc[(nm, k)] = plan_ag(f"{nm}{k}", dq_jobs[bnd - 1]["wr_cnt"])

        plan_dq(17)
        ag_cc[("g", 0)] = plan_ag("g0", dq_jobs[8]["wr_cnt"])
        ag_cc[("u", 0)] = plan_ag("u0", dq_jobs[16]["wr_cnt"])

        budget = [8] * 8 + [3] * 20 + [4] * 29 + [1] * 12 + [0] * 17
        assert 17 + sum(budget) == len(dq_jobs)

        fstate = {}
        dbg_hooks = {}

        def plan_x1T_half(xsrc, h, wait_pe=0):
            evc = {}
            tev = fstate.setdefault("tev", {})
            gb0 = fstate.get("gb0", 0)
            chunks = [0, 0, 1, 2, 3] if not fstate.get("ln_warm") else range(4)
            fstate["ln_warm"] = 1
            for mi, m in enumerate(chunks):
                rows = ts(h * 4 + m, 128)
                if mi >= 2:
                    P.wait("sync", "sVe", evc[f"apply{chunks[mi - 2]}"])
                P.op("sync", lambda e, rows=rows, m=m: e.dma_start(
                    out=v_xc[m % 2], in_=xsrc[rows, :]), "sXc", 16)
                xc_after = P.cnt("sXc")
                P.wait("vector", "sXc", xc_after)
                if m == 0 and wait_pe:
                    P.wait("vector", "sPE", wait_pe)
                P.wait("scalar", "sXc", xc_after)
                P.op("scalar", lambda e, m=m: e.activation(
                    out=v_scr, in_=v_xc[m % 2], func=AF.Copy,
                    accum_out=s_sm[:]), None)
                P.op("scalar", lambda e: e.mul(s_mu[:], s_sm[:], 1.0 / DM), None)
                P.op("scalar", lambda e: e.mul(s_nmu2[:], s_mu[:], -1.0), None)
                P.op("scalar", lambda e: e.mul(s_warm[:], s_warm[:], 1.0), None)
                P.op("scalar", lambda e, m=m: e.activation(
                    out=v_scr, in_=v_xc[m % 2], func=AF.Square,
                    bias=s_nmu2[:], accum_out=s_ss[:]), None)
                P.op("scalar", lambda e: e.mul(s_var[:], s_ss[:], 1.0 / DM), None)
                P.op("scalar", lambda e: e.add(s_var[:], s_var[:], s_eps[:]), None)
                P.op("scalar", lambda e: e.activation(
                    out=s_rstd[:], in_=s_var[:], func=AF.Ln), None)
                P.op("scalar", lambda e: e.activation(
                    out=s_rstd[:], in_=s_rstd[:], func=AF.Exp, scale=-0.5), None)
                sc = P.op("scalar", lambda e: e.drain(), "sSc")
                P.wait("vector", "sSc", sc)
                if s_stdbg is not None and xsrc is x_in and h == 0:
                    P.op("scalar", lambda e, m=m: e.copy(
                        s_stdbg[:, sl(m * 4 + 0, 1)], s_sm[:]), None)
                    P.op("scalar", lambda e, m=m: e.copy(
                        s_stdbg[:, sl(m * 4 + 1, 1)], s_ss[:]), None)
                    P.op("scalar", lambda e, m=m: e.copy(
                        s_stdbg[:, sl(m * 4 + 2, 1)], s_var[:]), None)
                    P.op("scalar", lambda e, m=m: e.copy(
                        s_stdbg[:, sl(m * 4 + 3, 1)], s_rstd[:]), None)
                P.op("vector", lambda e, m=m: e.tensor_scalar(
                    v_xnb, v_xc[m % 2], s_nmu2[:], s_rstd[:],
                    ALU.add, ALU.mult), None)
                evc[f"apply{m}"] = P.op("vector", lambda e: e.tensor_copy(
                    v_xnbb, v_xnb), "sVe")
                if m == 3 and dbg_xnb is not None and \
                        fstate.get("dumpxnb") is None:
                    fstate["dumpxnb"] = 1
                    P.wait("sync", "sVe", evc[f"apply{m}"])
                    P.op("sync", lambda e: e.dma_start(
                        out=dbg_xnb, in_=v_xnbb), "sSt", 16)
                P.wait("tensor", "sVe", evc[f"apply{m}"])
                for g in range(8):
                    gb = gb0 + mi * 8 + g
                    bank = gb % 2
                    if gb >= 2:
                        P.wait("tensor", "sSc", tev[gb - 2])

                    def tr4(e, g=g, bank=bank):
                        for t in range(4):
                            it = g * 4 + t
                            i = e.transpose(pbh[bank][:, ts(t, 128)],
                                            v_xnbb[:, ts(it, 128)], s_id[:])
                        return i
                    pe_c = P.op("tensor", tr4, "sPE")
                    P.wait("scalar", "sPE", pe_c)

                    def ev4(e, g=g, m=m, bank=bank):
                        return e.copy(
                            s_x1T[:].rearrange("p (it s) -> p it s", it=32)
                            [:, g * 4:g * 4 + 4, ts(m, 128)],
                            pbh[bank][:, 0:512].rearrange("p (t s) -> p t s", t=4))
                    tev[gb] = P.op("scalar", ev4, "sSc")
                P.wait("vector", "sPE", P.cnt("sPE"))
                if SERIAL >= 1:
                    P.fence()
            fstate["gb0"] = gb0 + 8 * len(list(chunks))
            return P.cnt("sSc")

        def plan_tlora(lA_tiles, out_tile, nit):
            P.wait("tensor", "sGp", lora_gp)
            def mm(e, lA=lA_tiles, nit=nit):
                for it in range(nit):
                    i = e.matmul(pb[3][0:16, :], lA[:, ts(it, R)],
                                 s_x1T[:, ts(it, 512)] if nit == 32 else
                                 s_x3T[:, ts(it, 512)],
                                 start=(it == 0), stop=(it == nit - 1))
                return i
            c = P.op("tensor", mm, "sPE")
            P.wait("scalar", "sPE", c)
            vc = P.op("scalar", lambda e, o=out_tile: e.copy(
                o, pb[3][0:16, :]), "sSc")
            P.wait("tensor", "sSc", vc)
            if dbg_tg is not None and out_tile is s_tg and \
                    fstate.get("dumptg") is None:
                fstate["dumptg"] = 1
                P.wait("sync", "sSc", vc)
                P.op("sync", lambda e: e.dma_start(out=dbg_tg, in_=s_tg),
                     "sSt", 16)
            return vc

        def plan_ffn(xsrc, x_out, xag_write, dq_budget):
            f = {}
            for h in range(2):
                wait_pe = fstate.get("last_down_pe", 0)
                x1t_sc = plan_x1T_half(xsrc, h, wait_pe)
                if h == 0 and fstate.get("dump1") is None:
                    fstate["dump1"] = 1
                    dump_x1t()
                    dump_wg()
                P.wait("tensor", "sSc", x1t_sc)
                plan_tlora(s_gA, s_tg, 32)
                plan_tlora(s_uA, s_tu, 32)
                for ot in range(NOT):
                    k = next(i for i, (a, b2) in enumerate(GU_CHUNKS)
                             if a <= ot < b2)
                    otl = ot - GU_CHUNKS[k][0]
                    P.wait("sync", "sCC", ag_cc[("g", k)])
                    P.wait("sync", "sPE", f.get(f"gmm{ot - 1}", 0))

                    def ldw(e, dst, srcnm, k=k, otl=otl):
                        h = P.sems["sGw"][0]
                        for r in range(8):
                            e.dma_start(
                                out=dst[:, ts(r, 512)].rearrange(
                                    "p (st o) -> p st o", st=4),
                                in_=hb[f"{srcnm}{k}_out"][r, otl].rearrange(
                                    "(st p) o -> p st o", p=128)
                            ).then_inc(h, 16)
                    P.emit("sync", lambda e, fn=ldw: fn(e, s_gw, "g"))
                    P.sems["sGw"][1] += 128
                    gw_after = P.cnt("sGw")
                    P.wait("sync", "sCC", ag_cc[("u", k)])
                    P.wait("sync", "sPE", f.get(f"umm{ot - 1}", 0))
                    P.emit("sync", lambda e, fn=ldw: fn(e, s_uw, "u"))
                    P.sems["sGw"][1] += 128
                    uw_after = P.cnt("sGw")
                    gbank, ubank = (0, 1) if ot % 2 == 0 else (2, 3)
                    P.wait("tensor", "sGw", gw_after)
                    P.wait("tensor", "sVe", f.get(f"x3{ot - 2}", 0))

                    def gmm(e, gbank=gbank, ot=ot):
                        for it in range(32):
                            e.matmul(pb[gbank][:], s_gw[:, ts(it, 128)],
                                     s_x1T[:, ts(it, 512)],
                                     start=(it == 0), stop=False)
                        return e.matmul(pb[gbank][:], s_gB[:, ts(ot, 128)],
                                        s_tg[:], start=False, stop=True)
                    f[f"gmm{ot}"] = P.op("tensor", gmm, "sPE")
                    P.wait("tensor", "sGw", uw_after)

                    def umm(e, ubank=ubank, ot=ot):
                        for it in range(32):
                            e.matmul(pb[ubank][:], s_uw[:, ts(it, 128)],
                                     s_x1T[:, ts(it, 512)],
                                     start=(it == 0), stop=False)
                        return e.matmul(pb[ubank][:], s_uB[:, ts(ot, 128)],
                                        s_tu[:], start=False, stop=True)
                    f[f"umm{ot}"] = P.op("tensor", umm, "sPE")
                    P.wait("scalar", "sPE", f[f"gmm{ot}"])
                    P.wait("scalar", "sVe", f.get(f"x3{ot - 2}", 0))
                    rc = P.op("scalar", lambda e, gbank=gbank, ot=ot: e.activation(
                        out=s_y1r[ot % 2][:], in_=pb[gbank][:], func=AF.Relu),
                        "sSc")
                    P.wait("vector", "sSc", rc)
                    P.wait("vector", "sPE", f[f"umm{ot}"])
                    if ot == 0 and h == 0 and dbg_y1 is not None \
                            and fstate.get("dumpy") is None:
                        fstate["dumpy"] = 1
                        vc0 = P.op("vector", lambda e: e.tensor_copy(
                            s_xpc[0][:], pb[0][:]), "sVe")
                        P.wait("sync", "sVe", vc0)
                        P.op("sync", lambda e: e.dma_start(
                            out=dbg_y1, in_=s_xpc[0][:]), "sSt", 16)
                    f[f"x3{ot}"] = P.op(
                        "vector", lambda e, ubank=ubank, ot=ot:
                        e.scalar_tensor_tensor(
                            out=s_x3T[:, ts(ot, 512)], in0=pb[ubank][:],
                            scalar=1.0, in1=s_y1r[ot % 2][:],
                            op0=ALU.mult, op1=ALU.mult), "sVe")
                    if ot == 0 and h == 0 and dbg_x3 is not None \
                            and fstate.get("dumpx3") is None:
                        fstate["dumpx3"] = 1
                        P.wait("sync", "sVe", f["x30"])
                        P.op("sync", lambda e: e.dma_start(
                            out=dbg_x3, in_=s_x3T[:, ts(0, 512)]), "sSt", 16)
                    if dq_budget:
                        plan_dq(dq_budget.pop(0))
                    if SERIAL >= 2 or (SERIAL >= 1 and ot % 8 == 7):
                        P.fence()
                P.wait("tensor", "sVe", f[f"x3{NOT - 1}"])
                plan_tlora(s_dA, s_t2, NOT)
                prev_ev = {}
                prev_st = {}
                prev_gs = {}
                for dmt in range(8):
                    half = 0 if dmt < 4 else 1
                    col0 = dmt * 512 - half * 2048
                    P.wait("sync", "sCC", ag_cc[("d", half)])
                    if dmt == 0 and dbg_wd is not None and \
                            fstate.get("dumpwd") is None:
                        fstate["dumpwd"] = 1
                        P.op("sync", lambda e: e.dma_start(
                            out=dbg_wd, in_=hb["d0_in"][:, 0:512]), "sSt", 16)
                    for ot in range(NOT):
                        wbuf = ot % 2
                        if ot >= 2:
                            P.wait("sync", "sPE", f.get(f"dmm{ot - 2}", 0))
                        else:
                            P.wait("sync", "sPE", f.get(f"dmm{NOT - 2 + ot}", 0))
                        P.op("sync", lambda e, wbuf=wbuf, half=half, col0=col0,
                             ot=ot: e.dma_start(
                                 out=s_wd[wbuf][:],
                                 in_=hb[f"d{half}_out"][sl(ot * 128, 128),
                                                        col0:col0 + 512]),
                             "sGw", 16)
                        wd_after = P.cnt("sGw")
                        P.wait("tensor", "sGw", wd_after)
                        if ot == 0 and dmt >= 1:
                            P.wait("tensor", "sVe", f[f"dev{dmt - 1}"])

                        def dmm(e, wbuf=wbuf, ot=ot, dmt=dmt):
                            for ssub in range(4):
                                i = e.matmul(pb[ssub][:],
                                             s_x3T[:, sl(ot * 512 + ssub * 128,
                                                         128)],
                                             s_wd[wbuf][:],
                                             start=(ot == 0), stop=False)
                            return i
                        f[f"dmm{ot}"] = P.op("tensor", dmm, "sPE")

                    def dlora(e, dmt=dmt):
                        for ssub in range(4):
                            i = e.matmul(pb[ssub][:], s_t2[:, ts(ssub, 128)],
                                         s_dB[:, ts(dmt, 512)],
                                         start=False, stop=True)
                        return i
                    mm_c = P.op("tensor", dlora, "sPE")
                    if SERIAL >= 2:
                        P.fence()
                    if dmt == 0 and dbg_y3 is not None and \
                            fstate.get("dumpy3") is None:
                        fstate["dumpy3"] = 1
                        P.wait("vector", "sPE", mm_c)
                        vy3 = P.op("vector", lambda e: e.tensor_copy(
                            s_dbgst[:], pb[0][:]), "sVe")
                        P.wait("sync", "sVe", vy3)
                        P.op("sync", lambda e: e.dma_start(
                            out=dbg_y3, in_=s_dbgst[:]), "sSt", 16)
                    for ssub in range(4):
                        rows = ts(h * 4 + ssub, 128)
                        bb = ssub % 2
                        P.wait("sync", "sVe", prev_ev.get(bb, 0))
                        P.op("sync", lambda e, rows=rows, dmt=dmt, bb=bb:
                             e.dma_start(out=s_xrp[bb][:],
                                         in_=xsrc[rows, ts(dmt, 512)]),
                             "sXc", 16)
                        xr_after = P.cnt("sXc")
                        P.wait("vector", "sPE", mm_c)
                        P.wait("vector", "sXc", xr_after)
                        P.wait("vector", "sSt", prev_st.get(bb, 0))
                        P.wait("vector", "sGs", prev_gs.get(bb, 0))
                        ev = P.op("vector", lambda e, ssub=ssub, bb=bb:
                                  e.scalar_tensor_tensor(
                                      out=s_xpc[bb][:], in0=pb[ssub][:],
                                      scalar=0.5, in1=s_xrp[bb][:],
                                      op0=ALU.mult, op1=ALU.add), "sVe")
                        prev_ev[bb] = ev
                        if ssub == 3:
                            f[f"dev{dmt}"] = ev
                        P.wait("sync", "sVe", ev)
                        P.op("sync", lambda e, rows=rows, dmt=dmt, bb=bb:
                             e.dma_start(out=x_out[rows, ts(dmt, 512)],
                                         in_=s_xpc[bb][:]), "sSt", 16)
                        prev_st[bb] = P.cnt("sSt")
                        if xag_write is not None:
                            P.wait("gpsimd", "sVe", ev)
                            P.op("gpsimd", lambda e, rows=rows, dmt=dmt, bb=bb:
                                 e.dma_start(out=xag_write[rows, ts(dmt, 512)],
                                             in_=s_xpc[bb][:]), "sGs", 16)
                            prev_gs[bb] = P.cnt("sGs")
                fstate["last_down_pe"] = P.cnt("sPE")
                if SERIAL >= 1:
                    P.fence()
                if xag_write is not None:
                    st_cnt = P.cnt("sGs")
                    P.wait("gpsimd", "sGs", st_cnt)
                    src = xag_in[ts(h, 512), :]
                    dst = xag_out[h]
                    P.op("gpsimd", lambda e, src=src, dst=dst:
                         e.collective_compute(
                             "AllGather", ALU.bypass,
                             replica_groups=[list(range(NCORES))],
                             ins=[src.flatten().unsqueeze(0)],
                             outs=[dst.flatten().unsqueeze(0)]), "sCC")
                    ag_cc[("x", h)] = P.cnt("sCC")

        dbg_wg = dbg_tensor("wg0", [8 * 8 * 512, 128], BF16)
        dbg_x1t = dbg_tensor("x1t", [128, 32 * 512], BF16)
        dbg_y1 = dbg_tensor("y1", [128, 512], F32)
        dbg_st = dbg_tensor("st", [128, 16], F32)
        dbg_wd = dbg_tensor("wd0", [1376, 512], BF16)
        dbg_tg = dbg_tensor("tg", [16, 512], BF16)
        dbg_xnb = dbg_tensor("xnb", [128, DM], BF16)
        dbg_y3 = dbg_tensor("y3", [128, 512], F32)
        s_dbgst = sbuf("s_dbgst", [128, 512], F32) if dbg_y3 is not None else None
        s_stdbg = sbuf("s_stdbg", [128, 16], F32) if dbg_st is not None else None
        dbg_x3 = dbg_tensor("x3", [128, 512], BF16)

        def dump_x1t():
            if dbg_x1t is None:
                return
            P.wait("sync", "sSc", P.cnt("sSc"))
            P.op("sync", lambda e: e.dma_start(out=dbg_x1t, in_=s_x1T[:]),
                 "sSt", 16)
            if dbg_st is not None:
                P.wait("sync", "sVe", P.cnt("sVe"))
                P.op("sync", lambda e: e.dma_start(out=dbg_st, in_=s_stdbg[:]),
                     "sSt", 16)

        def dump_wg():
            if dbg_wg is None:
                return
            P.wait("sync", "sCC", P.cnt("sCC"))
            P.op("sync", lambda e: e.dma_start(
                out=dbg_wg[0:8 * 512, :],
                in_=hb["g0_in"].rearrange("n i o -> (n i) o")), "sSt", 16)
            P.op("sync", lambda e: e.dma_start(
                out=dbg_wg[8 * 512:2 * 8 * 512, :],
                in_=hb["g0_out"][0].rearrange("n i o -> (n i) o")), "sSt", 16)

        def dump_y1(ot_target):
            if dbg_y1 is None:
                return None
            return ot_target


        def plan_convert(src_ap, n_tiles, dst_fn, wr_sem="sWr"):
            pass

        def plan_wqkv_convert():
            cvt = {}
            for et in range(12):
                b = et % 2
                if et >= 2:
                    P.wait("sync", "sSc", cvt[f"ev{et - 2}"])
                P.op("sync", lambda e, et=et: e.dma_start(
                    out=q_cvf, in_=wqkv_sl[ts(et, 128), :]), "sXc", 16)
                xa = P.cnt("sXc")
                P.wait("vector", "sXc", xa)
                if et >= 2:
                    P.wait("vector", "sSc", cvt[f"ev{et - 2}"])
                vc = P.op("vector", lambda e: e.tensor_copy(q_cvb, q_cvf), "sVe")
                P.wait("tensor", "sVe", vc)
                gb0 = fstate.get("gb0", 0)
                for g in range(8):
                    gb = gb0 + g
                    bank = gb % 2
                    tev = fstate["tev"]
                    if gb >= 2:
                        P.wait("tensor", "sSc", tev[gb - 2])

                    def tr4(e, g=g, bank=bank):
                        for t in range(4):
                            it = g * 4 + t
                            i = e.transpose(pbh[bank][:, ts(t, 128)],
                                            q_cvb[:, ts(it, 128)], s_id[:])
                        return i
                    pe_c = P.op("tensor", tr4, "sPE")
                    P.wait("scalar", "sPE", pe_c)

                    def ev4(e, g=g, bank=bank):
                        return e.copy(
                            q_cvst[:].rearrange("p (t s) -> p t s", t=32)
                            [:, g * 4:g * 4 + 4, :],
                            pbh[bank][:, 0:512].rearrange("p (t s) -> p t s", t=4))
                    tev[gb] = P.op("scalar", ev4, "sSc")
                fstate["gb0"] = gb0 + 8
                cvt[f"ev{et}"] = P.cnt("sSc")
                P.wait("sync", "sSc", cvt[f"ev{et}"])
                P.op("sync", lambda e, et=et: e.dma_start(
                    out=wqkvT_loc[:, ts(et, 128)].rearrange(
                        "(t p) e -> p t e", p=128),
                    in_=q_cvst[:].rearrange("p (t e) -> p t e", e=128)),
                    "sWr", 16)

        def plan_wout_convert():
            cvt = {}
            for et in range(32):
                if et >= 2:
                    P.wait("sync", "sSc", cvt[f"ev{et - 2}"])
                P.op("sync", lambda e, et=et: e.dma_start(
                    out=q_cvf[:, 0:512], in_=wout_sl[ts(et, 128), :]), "sXc", 16)
                xa = P.cnt("sXc")
                P.wait("vector", "sXc", xa)
                if et >= 2:
                    P.wait("vector", "sSc", cvt[f"ev{et - 2}"])
                vc = P.op("vector", lambda e: e.tensor_copy(
                    q_cvb[:, 0:512], q_cvf[:, 0:512]), "sVe")
                P.wait("tensor", "sVe", vc)
                gb0 = fstate.get("gb0", 0)
                tev = fstate["tev"]
                gb = gb0
                bank = gb % 2
                if gb >= 2:
                    P.wait("tensor", "sSc", tev[gb - 2])

                def tr4(e, bank=bank):
                    for t in range(4):
                        i = e.transpose(pbh[bank][:, ts(t, 128)],
                                        q_cvb[:, ts(t, 128)], s_id[:])
                    return i
                pe_c = P.op("tensor", tr4, "sPE")
                P.wait("scalar", "sPE", pe_c)
                tev[gb] = P.op("scalar", lambda e, bank=bank: e.copy(
                    q_cvst[:, 0:512], pbh[bank][:, 0:512]), "sSc")
                fstate["gb0"] = gb0 + 1
                cvt[f"ev{et}"] = P.cnt("sSc")
                P.wait("sync", "sSc", cvt[f"ev{et}"])
                P.op("sync", lambda e, et=et: e.dma_start(
                    out=hb["wout_in"][:, ts(et, 128)].rearrange(
                        "(t p) e -> p t e", p=128),
                    in_=q_cvst[:, 0:512].rearrange("p (t e) -> p t e", e=128)),
                    "sWr", 16)
            fstate["wout_wr"] = P.cnt("sWr")

        def plan_qkv():
            P.wait("sync", "sWr", P.cnt("sWr"))
            for it in range(32):
                P.op("sync", lambda e, it=it: e.dma_start(
                    out=q_wqv[:, sl(it * 1536, 1536)],
                    in_=wqkvT_loc[ts(it, 128), :]), "sGw", 16)
            wqv_ld = P.cnt("sGw")
            for i, srcap in enumerate((qlnw, qlnb, klnw, klnb)):
                P.op("sync", lambda e, i=i, s=srcap: e.dma_start(
                    out=q_bc[i], in_=s[None, :].broadcast_to([128, ESL])),
                    "sLd", 16)
            bc_ld = P.cnt("sLd")
            P.wait("vector", "sLd", bc_ld)
            P.op("vector", lambda e: e.tensor_scalar(
                q_bc[0], q_bc[0], SCALE, None, ALU.mult), None)
            bcs = P.op("vector", lambda e: e.tensor_scalar(
                q_bc[1], q_bc[1], SCALE, None, ALU.mult), "sVe")
            P.wait("tensor", "sGw", wqv_ld)
            ev_prev = {}
            st_prev = {}
            gs_prev = {}
            x1aT_free = {}
            for m in range(64):
                b = m % 2
                r, hh, row0 = m // 8, (m % 8) // 4, (m % 4) * 128
                if m >= 2:
                    P.wait("sync", "sVe", ev_prev.get(b, 0))
                P.op("sync", lambda e, b=b, r=r, hh=hh, row0=row0: e.dma_start(
                    out=q_xagc[b], in_=xag_out[hh][r, sl(row0, 128), :]),
                    "sXc", 16)
                xa = P.cnt("sXc")
                P.wait("scalar", "sXc", xa)
                if m >= 2:
                    P.wait("scalar", "sVe", ev_prev.get(b, 0))
                P.op("scalar", lambda e, b=b: e.activation(
                    out=q_apb, in_=q_xagc[b], func=AF.Copy,
                    accum_out=s_sm[:]), None)
                P.op("scalar", lambda e: e.mul(s_mu[:], s_sm[:], 1.0 / DM), None)
                P.op("scalar", lambda e: e.mul(s_nmu2[:], s_mu[:], -1.0), None)
                P.op("scalar", lambda e: e.mul(s_warm[:], s_warm[:], 1.0), None)
                P.op("scalar", lambda e, b=b: e.activation(
                    out=q_apb, in_=q_xagc[b], func=AF.Square,
                    bias=s_nmu2[:], accum_out=s_ss[:]), None)
                P.op("scalar", lambda e: e.mul(s_var[:], s_ss[:], 1.0 / DM), None)
                P.op("scalar", lambda e: e.add(s_var[:], s_var[:], s_eps[:]), None)
                P.op("scalar", lambda e: e.activation(
                    out=s_rstd[:], in_=s_var[:], func=AF.Ln), None)
                P.op("scalar", lambda e: e.activation(
                    out=s_rstd[:], in_=s_rstd[:], func=AF.Exp, scale=-0.5), None)
                scm = P.op("scalar", lambda e: e.drain(), "sSc")
                P.wait("vector", "sSc", scm)
                if m == 0:
                    P.wait("vector", "sVe", bcs)
                P.op("vector", lambda e, b=b: e.tensor_scalar(
                    q_apf, q_xagc[b], s_nmu2[:], s_rstd[:],
                    ALU.add, ALU.mult), None)
                apl = P.op("vector", lambda e: e.tensor_copy(q_apb, q_apf), "sVe")
                P.wait("tensor", "sVe", apl)
                P.wait("tensor", "sSc", x1aT_free.get(b, 0))
                gb0 = fstate["gb0"]
                tev = fstate["tev"]
                for g in range(8):
                    gb = gb0 + g
                    bank = gb % 2
                    if gb >= 2:
                        P.wait("tensor", "sSc", tev[gb - 2])

                    def tr4(e, g=g, bank=bank):
                        for t in range(4):
                            it = g * 4 + t
                            i = e.transpose(pbh[bank][:, ts(t, 128)],
                                            q_apb[:, ts(it, 128)], s_id[:])
                        return i
                    pe_c = P.op("tensor", tr4, "sPE")
                    P.wait("scalar", "sPE", pe_c)

                    def ev4(e, g=g, b=b, bank=bank):
                        return e.copy(
                            q_x1aT[b][:].rearrange("p (t s) -> p t s", t=32)
                            [:, g * 4:g * 4 + 4, :],
                            pbh[bank][:, 0:512].rearrange("p (t s) -> p t s",
                                                          t=4))
                    tev[gb] = P.op("scalar", ev4, "sSc")
                fstate["gb0"] = gb0 + 8
                x1t_done = P.cnt("sSc")
                P.wait("vector", "sPE", P.cnt("sPE"))
                P.wait("tensor", "sSc", x1t_done)
                if m >= 2:
                    P.wait("tensor", "sVe", ev_prev.get(b, 0))

                def qmm(e, b=b):
                    for et in range(3):
                        for it in range(32):
                            i = e.matmul(pb[et][:],
                                         q_x1aT[b][:, ts(it, 128)],
                                         q_wqv[:, it * 1536 + et * 512:
                                               it * 1536 + et * 512 + 512],
                                         start=(it == 0), stop=(it == 31))
                    return i
                mmc = P.op("tensor", qmm, "sPE")
                P.wait("vector", "sPE", mmc)
                P.wait("vector", "sSt", st_prev.get(b, 0))
                P.wait("vector", "sGs", gs_prev.get(b, 0))
                for et in range(3):
                    acc = q_stat[:, sl(4 * m + et, 1)] if et < 2 else None
                    i = P.op("vector", lambda e, b=b, et=et, acc=acc:
                             e.tensor_scalar(
                                 q_qkvc[b][:, ts(et, 512)], pb[et][:],
                                 -CLIP, CLIP, ALU.max, ALU.min,
                                 accum_out=acc), "sVe")
                ev_prev[b] = i
                P.wait("scalar", "sVe", ev_prev[b])
                P.op("scalar", lambda e, b=b, m=m: e.activation(
                    out=q_apb[:, 0:512], in_=q_qkvc[b][:, 0:512],
                    func=AF.Square, accum_out=q_stat[:, sl(4 * m + 2, 1)]),
                    None)
                sq2 = P.op("scalar", lambda e, b=b, m=m: e.activation(
                    out=q_apb[:, 0:512], in_=q_qkvc[b][:, 512:1024],
                    func=AF.Square, accum_out=q_stat[:, sl(4 * m + 3, 1)]),
                    "sSc")
                x1aT_free[b] = x1t_done
                P.wait("sync", "sVe", ev_prev[b])
                P.op("sync", lambda e, b=b, m=m: e.dma_start(
                    out=qk_stash[ts(m, 128), :], in_=q_qkvc[b][:, 0:1024]),
                    "sSt", 16)
                st_prev[b] = P.cnt("sSt")
                P.wait("gpsimd", "sVe", ev_prev[b])
                P.op("gpsimd", lambda e, b=b, m=m: e.dma_start(
                    out=vhat[ts(m, 128), :], in_=q_qkvc[b][:, 1024:1536]),
                    "sGs", 16)
                gs_prev[b] = P.cnt("sGs")
            P.wait("sync", "sSc", P.cnt("sSc"))
            P.wait("sync", "sVe", P.cnt("sVe"))
            P.op("sync", lambda e: e.dma_start(out=stats_cc, in_=q_stat),
                 "sSt", 16)
            P.wait("gpsimd", "sSt", P.cnt("sSt"))
            P.op("gpsimd", lambda e: e.collective_compute(
                "AllReduce", ALU.add, replica_groups=[list(range(NCORES))],
                ins=[stats_cc.flatten().unsqueeze(0)],
                outs=[stats_cc.flatten().unsqueeze(0)]), "sCC")
            ccv = P.cnt("sCC")
            P.wait("sync", "sCC", ccv)
            P.op("sync", lambda e: e.dma_start(out=q_statR, in_=stats_cc),
                 "sLd", 16)
            stR = P.cnt("sLd")
            muq, nmuq, rsq = (q_mu[:, sl(0, 64)], q_mu[:, sl(64, 64)],
                              q_mu[:, sl(128, 64)])
            muk, nmuk, rsk = (q_mu[:, sl(192, 64)], q_mu[:, sl(256, 64)],
                              q_mu[:, sl(320, 64)])
            P.wait("vector", "sLd", stR)
            qs = q_statR[:].rearrange("p (m f) -> p m f", f=4)
            P.op("vector", lambda e: e.tensor_scalar(
                muq, qs[:, :, 0], 1.0 / DM, None, ALU.mult), None)
            P.op("vector", lambda e: e.tensor_scalar(
                nmuq, muq, -1.0, None, ALU.mult), None)
            P.op("vector", lambda e: e.tensor_scalar(
                muk, qs[:, :, 1], 1.0 / DM, None, ALU.mult), None)
            vm = P.op("vector", lambda e: e.tensor_scalar(
                nmuk, muk, -1.0, None, ALU.mult), "sVe")
            P.wait("scalar", "sVe", vm)
            P.op("scalar", lambda e: e.activation(
                out=rsq, in_=muq, func=AF.Square), None)
            P.op("scalar", lambda e: e.mul(rsq, rsq, -1.0), None)
            sc1 = P.op("scalar", lambda e: e.drain(), "sSc")
            P.wait("vector", "sSc", sc1)
            P.op("vector", lambda e: e.scalar_tensor_tensor(
                out=rsq, in0=qs[:, :, 2], scalar=1.0 / DM, in1=rsq,
                op0=ALU.mult, op1=ALU.add), None)
            vq = P.op("vector", lambda e: e.tensor_scalar(
                rsq, rsq, 1.0, LN_EPS, ALU.mult, ALU.add), "sVe")
            P.wait("scalar", "sVe", vq)
            P.op("scalar", lambda e: e.activation(out=rsq, in_=rsq,
                                                  func=AF.Ln), None)
            P.op("scalar", lambda e: e.activation(
                out=rsq, in_=rsq, func=AF.Exp, scale=-0.5), None)
            P.op("scalar", lambda e: e.activation(
                out=rsk, in_=muk, func=AF.Square), None)
            P.op("scalar", lambda e: e.mul(rsk, rsk, -1.0), None)
            sc2 = P.op("scalar", lambda e: e.drain(), "sSc")
            P.wait("vector", "sSc", sc2)
            P.op("vector", lambda e: e.scalar_tensor_tensor(
                out=rsk, in0=qs[:, :, 3], scalar=1.0 / DM, in1=rsk,
                op0=ALU.mult, op1=ALU.add), None)
            vk = P.op("vector", lambda e: e.tensor_scalar(
                rsk, rsk, 1.0, LN_EPS, ALU.mult, ALU.add), "sVe")
            P.wait("scalar", "sVe", vk)
            P.op("scalar", lambda e: e.activation(out=rsk, in_=rsk,
                                                  func=AF.Ln), None)
            P.op("scalar", lambda e: e.activation(
                out=rsk, in_=rsk, func=AF.Exp, scale=-0.5), None)
            scM = P.op("scalar", lambda e: e.drain(), "sSc")
            ld_prev = {}
            for m in range(64):
                b = m % 2
                if m >= 2:
                    P.wait("sync", "sVe", ld_prev.get(b, 0))
                P.op("sync", lambda e, b=b, m=m: e.dma_start(
                    out=q_kld[b], in_=qk_stash[ts(m, 128), :]), "sXc", 16)
                xa = P.cnt("sXc")
                P.wait("vector", "sXc", xa)
                if m == 0:
                    P.wait("vector", "sSc", scM)
                P.wait("vector", "sSt", st_prev.get(f"p2{b}", 0))
                P.op("vector", lambda e, b=b, m=m: e.tensor_scalar(
                    q_apf[:, 0:512], q_kld[b][:, 0:512],
                    nmuq[:, sl(m, 1)], rsq[:, sl(m, 1)],
                    ALU.add, ALU.mult), None)
                P.op("vector", lambda e: e.scalar_tensor_tensor(
                    out=q_apf[:, 512:1024], in0=q_apf[:, 0:512], scalar=1.0,
                    in1=q_bc[0], op0=ALU.mult, op1=ALU.mult), None)
                P.op("vector", lambda e: e.tensor_tensor(
                    q_apb[:, 0:512], q_apf[:, 512:1024], q_bc[1],
                    ALU.add), None)
                P.op("vector", lambda e, b=b, m=m: e.tensor_scalar(
                    q_apf[:, 0:512], q_kld[b][:, 512:1024],
                    nmuk[:, sl(m, 1)], rsk[:, sl(m, 1)],
                    ALU.add, ALU.mult), None)
                P.op("vector", lambda e: e.scalar_tensor_tensor(
                    out=q_apf[:, 512:1024], in0=q_apf[:, 0:512], scalar=1.0,
                    in1=q_bc[2], op0=ALU.mult, op1=ALU.mult), None)
                ldv = P.op("vector", lambda e: e.tensor_tensor(
                    q_apb[:, 512:1024], q_apf[:, 512:1024], q_bc[3],
                    ALU.add), "sVe")
                ld_prev[b] = ldv
                P.wait("sync", "sVe", ldv)
                P.op("sync", lambda e, m=m: e.dma_start(
                    out=qhat[ts(m, 128), :], in_=q_apb[:, 0:512]), "sSt", 16)
                P.op("sync", lambda e, m=m: e.dma_start(
                    out=khat[ts(m, 128), :], in_=q_apb[:, 512:1024]),
                    "sSt", 16)
                st_prev[f"p2{b}"] = P.cnt("sSt")

        a_S = vf(0, SEQ)
        a_P = vb(16384, SEQ)
        a_PT = vb(24576, SEQ)
        a_B = vf(16384, 8704)
        a_K = vb(50176, SEQ)
        a_Q = vb(58368, SEQ)
        a_V = vb(66560, 64 * 129)
        a_O = vb(74816, 128)
        a_OT = vb(74944, 128)
        a_l = vf(37536, 2)

        def plan_attention():
            for hh in range(4):
                P.op("sync", lambda e, hh=hh: e.dma_start(
                    out=a_K, in_=khat[:, ts(hh, 128)], transpose=True),
                    "sGw", 16)
                P.op("sync", lambda e, hh=hh: e.dma_start(
                    out=a_Q, in_=qhat[:, ts(hh, 128)], transpose=True),
                    "sGw", 16)
                P.op("sync", lambda e, hh=hh: e.dma_start(
                    out=a_V.rearrange("p (kb e) -> p kb e", e=129)[:, :, 0:128],
                    in_=vhat[:, ts(hh, 128)].rearrange(
                        "(kb p) d -> p kb d", p=128)), "sGw", 16)
                kvq_ld = P.cnt("sGw")
                vo = P.op("vector", lambda e: e.memset(
                    a_V.rearrange("p (kb e) -> p kb e", e=129)[:, :, 128:129],
                    1.0), "sVe")
                gio = P.op("gpsimd", lambda e: e.iota(
                    a_B, pattern=[[1, 8704]], base=-8064,
                    channel_multiplier=-1,
                    allow_small_or_imprecise_dtypes=True), "sGp", 1)
                gsel = P.op("gpsimd", lambda e: e.affine_select(
                    a_B, a_B, pattern=[[-1, 8704]], compare_op=ALU.is_ge,
                    fill=NEG_BIG, base=8064, channel_multiplier=1), "sGp", 1)
                P.wait("vector", "sGp", gsel)
                if hh > 0:
                    pass
                bmul = P.op("vector", lambda e, hh=hh: e.tensor_scalar(
                    a_B, a_B, s_slope[:, hh:hh + 1], None, ALU.mult), "sVe")
                P.wait("tensor", "sGw", kvq_ld)
                P.wait("tensor", "sVe", bmul)
                P.wait("tensor", "sVe", vo)
                for qt in range(64):
                    nkt = qt // 4 + 1
                    W = 512 * nkt
                    for kt in range(nkt):
                        bank = kt % 2
                        if kt >= 2:
                            P.wait("tensor", "sVe", fstate[f"attn_ev{bank}"])
                        def smm(e, qt=qt, kt=kt, bank=bank):
                            return e.matmul(pb[bank][:], a_Q[:, ts(qt, 128)],
                                            a_K[:, ts(kt, 512)],
                                            start=True, stop=True)
                        pe_s = P.op("tensor", smm, "sPE")
                        P.wait("vector", "sPE", pe_s)
                        off = kt * 512 - qt * 128 + 8064
                        fstate[f"attn_ev{bank}"] = P.op(
                            "vector", lambda e, kt=kt, off=off, bank=bank:
                            e.scalar_tensor_tensor(
                                out=a_S[:, ts(kt, 512)], in0=pb[bank][:],
                                scalar=1.0, in1=a_B[:, sl(off, 512)],
                                op0=ALU.mult, op1=ALU.add), "sVe")
                    nm = P.op("vector", lambda e, W=W: e.tensor_reduce(
                        s_negmax[:], a_S[:, 0:W], axis=AX.X, op=ALU.max,
                        negate=True), "sVe")
                    P.wait("scalar", "sVe", nm)
                    if qt > 0 or hh > 0:
                        P.wait("scalar", "sPE", fstate.get("attn_pt_done", 0))
                    exq = P.op("scalar", lambda e, W=W: e.activation(
                        out=a_P[:, 0:W], in_=a_S[:, 0:W], func=AF.Exp,
                        bias=s_negmax[:]), "sSc")
                    P.wait("tensor", "sSc", exq)
                    gb0 = fstate["gb0"]
                    tev = fstate["tev"]
                    nb4 = nkt
                    for g in range(nb4):
                        gb = gb0 + g
                        bank = gb % 2
                        if gb >= 2:
                            P.wait("tensor", "sSc", tev[gb - 2])

                        def tr4(e, g=g, bank=bank):
                            for t in range(4):
                                kb = g * 4 + t
                                i = e.transpose(pbh[bank][:, ts(t, 128)],
                                                a_P[:, ts(kb, 128)], s_id[:])
                            return i
                        pe_c = P.op("tensor", tr4, "sPE")
                        P.wait("scalar", "sPE", pe_c)

                        def ev4(e, g=g, bank=bank):
                            return e.copy(a_PT[:, ts(g, 512)],
                                          pbh[bank][:, 0:512])
                        tev[gb] = P.op("scalar", ev4, "sSc")
                    fstate["gb0"] = gb0 + nb4
                    pt_sc = P.cnt("sSc")
                    fstate["attn_pt_done"] = P.cnt("sPE")
                    P.wait("tensor", "sSc", pt_sc)

                    def pv(e, nkt=nkt):
                        for kb in range(nkt * 4):
                            i = e.matmul(pb[2][:, 0:129],
                                         a_PT[:, ts(kb, 128)],
                                         a_V[:, sl(kb * 129, 129)],
                                         start=(kb == 0),
                                         stop=(kb == nkt * 4 - 1))
                        return i
                    pv_c = P.op("tensor", pv, "sPE")
                    P.wait("scalar", "sPE", pv_c)
                    lc = P.op("scalar", lambda e: e.copy(
                        a_l[:, 0:1], pb[2][:, 128:129]), "sSc")
                    P.wait("vector", "sSc", lc)
                    lr = P.op("vector", lambda e: e.reciprocal(
                        a_l[:, 1:2], a_l[:, 0:1]), "sVe")
                    P.wait("scalar", "sVe", lr)
                    oc = P.op("scalar", lambda e: e.activation(
                        out=a_O, in_=pb[2][:, 0:128], func=AF.Copy,
                        scale=a_l[:, 1:2]), "sSc")
                    P.wait("tensor", "sSc", oc)
                    gb = fstate["gb0"]
                    bank = gb % 2
                    if gb >= 2:
                        P.wait("tensor", "sSc", tev[gb - 2])
                    toc = P.op("tensor", lambda e, bank=bank: e.transpose(
                        pbh[bank][:, 0:128], a_O, s_id[:]), "sPE")
                    P.wait("scalar", "sPE", toc)
                    tev[gb] = P.op("scalar", lambda e, bank=bank: e.copy(
                        a_OT, pbh[bank][:, 0:128]), "sSc")
                    fstate["gb0"] = gb + 1
                    otc = P.cnt("sSc")
                    P.wait("sync", "sSc", otc)
                    kch = qt // 16
                    P.op("sync", lambda e, hh=hh, qt=qt, kch=kch: e.dma_start(
                        out=ot_in[kch][ts(hh, 128),
                                       sl((qt - kch * 16) * 128, 128)],
                        in_=a_OT), "sSt", 16)
                    P.wait("tensor", "sSc", P.cnt("sSc") - 1)
                P.fence()

        w_lhs = [vb(t * 8192, 8192) for t in range(4)]
        w_rhs = [vb(32768 + t * 4096, 4096) for t in range(4)]
        w_ev = [vf(24576, 1024), vf(25600, 1024)]
        w_aa = [vf(26624, 1024), vf(27648, 1024)]
        w_ab = [vf(28672, 1024), vf(29696, 1024)]
        w_ao = [vf(30720, 1024), vf(31744, 1024)]

        def plan_wout():
            for t in range(4):
                for k in range(4):
                    P.op("sync", lambda e, t=t, k=k: e.dma_start(
                        out=w_lhs[t][:, ts(k, 2048)], in_=ot_in[k][ts(t, 128), :]),
                        "sGw", 16)
                P.op("sync", lambda e, t=t: e.dma_start(
                    out=w_rhs[t], in_=hb["wout_in"][ts(t, 128), :]), "sGw", 16)
            wld = P.cnt("sGw")
            P.wait("tensor", "sGw", wld)
            ev_v, ev_s, st_c = {}, {}, {}
            for jq in range(4):
                for m in range(64):
                    pair = m % 2
                    if m >= 2 or jq > 0:
                        P.wait("tensor", "sVe", ev_v.get(pair, 0))
                        P.wait("tensor", "sSc", ev_s.get(pair, 0))

                    def wmm(e, m=m, jq=jq, pair=pair):
                        for t in range(4):
                            for jh in range(2):
                                i = e.matmul(
                                    pb[pair * 2 + jh][:],
                                    w_lhs[t][:, ts(m, 128)],
                                    w_rhs[t][:, sl(jq * 1024 + jh * 512, 512)],
                                    start=(t == 0), stop=(t == 3))
                        return i
                    mmc = P.op("tensor", wmm, "sPE")
                    P.wait("vector", "sPE", mmc)
                    P.wait("vector", "sSt", st_c.get(pair, 0))
                    ev_v[pair] = P.op("vector", lambda e, pair=pair: e.tensor_copy(
                        w_ev[pair][:, 0:512], pb[pair * 2][:]), "sVe")
                    P.wait("scalar", "sPE", mmc)
                    P.wait("scalar", "sSt", st_c.get(pair, 0))
                    ev_s[pair] = P.op("scalar", lambda e, pair=pair: e.copy(
                        w_ev[pair][:, 512:1024], pb[pair * 2 + 1][:]), "sSc")
                    P.wait("sync", "sVe", ev_v[pair])
                    P.wait("sync", "sSc", ev_s[pair])
                    P.op("sync", lambda e, m=m, jq=jq, pair=pair: e.dma_start(
                        out=rs_in[jq][ts(m, 128), :], in_=w_ev[pair]), "sSt", 16)
                    st_c[pair] = P.cnt("sSt")
            P.wait("gpsimd", "sSt", P.cnt("sSt"))
            for jq in range(4):
                if jq > 0:
                    P.wait("gpsimd", "sCC", ag_cc[("rs", jq - 1)])
                P.op("gpsimd", lambda e, jq=jq: e.collective_compute(
                    "ReduceScatter", ALU.add,
                    replica_groups=[list(range(NCORES))],
                    ins=[rs_in[jq].flatten().unsqueeze(0)],
                    outs=[rs_out[jq].flatten().unsqueeze(0)]), "sCC")
                ag_cc[("rs", jq)] = P.cnt("sCC")
            add_c, ao_st = {}, {}
            for jq in range(4):
                P.wait("sync", "sCC", ag_cc[("rs", jq)])
                for mb in range(8):
                    bb = mb % 2
                    P.wait("sync", "sVe", add_c.get(bb, 0))
                    P.op("sync", lambda e, jq=jq, mb=mb, bb=bb: e.dma_start(
                        out=w_aa[bb], in_=rs_out[jq][ts(mb, 128), :]), "sXc", 16)
                    P.op("sync", lambda e, jq=jq, mb=mb, bb=bb: e.dma_start(
                        out=w_ab[bb], in_=xres1[ts(mb, 128), sl(jq * 1024, 1024)]),
                        "sXc", 16)
                    xa = P.cnt("sXc")
                    P.wait("vector", "sXc", xa)
                    P.wait("vector", "sSt", ao_st.get(bb, 0))
                    add_c[bb] = P.op("vector", lambda e, bb=bb: e.tensor_tensor(
                        w_ao[bb], w_aa[bb], w_ab[bb], ALU.add), "sVe")
                    P.wait("sync", "sVe", add_c[bb])
                    P.op("sync", lambda e, jq=jq, mb=mb, bb=bb: e.dma_start(
                        out=xres2[ts(mb, 128), sl(jq * 1024, 1024)],
                        in_=w_ao[bb]), "sSt", 16)
                    ao_st[bb] = P.cnt("sSt")

        dbg_xp1 = dbg_tensor("xp1", [SLOC, DM], F32)
        ffn1_out = dbg_xp1 if (stage == 1 and dbg_xp1 is not None) else xres1
        plan_ffn(x_in, ffn1_out, xag_in if stage >= 2 else None, budget)

        if stage >= 2:
            P.fence()
            plan_wqkv_convert()
            plan_wout_convert()
            P.fence()
            plan_qkv()
        if stage >= 4:
            P.fence()
            plan_attention()
            P.fence()
            dbg_hooks["rsi"] = dbg_tensor("rsi", [SEQ, 1024], F32)
            plan_wout()
            dbg_xr2 = dbg_tensor("xr2", [SLOC, DM], F32)
            dbg_wsl = dbg_tensor("wsl", [512, DM], BF16)
            dbg_xr1 = dbg_tensor("xr1", [SLOC, DM], F32)
            dbg_ot4 = dbg_tensor("ot", [ESL, SEQ], BF16)
            dbg_rso = dbg_tensor("rso", [SLOC, 1024], F32)
            if dbg_rso is not None:
                P.wait("sync", "sCC", P.cnt("sCC"))
                P.op("sync", lambda e: e.dma_start(out=dbg_rso, in_=rs_out[1]),
                     "sSt", 16)
            if dbg_xr2 is not None:
                P.wait("sync", "sSt", P.cnt("sSt"))
                P.wait("sync", "sWr", P.cnt("sWr"))
                P.op("sync", lambda e: e.dma_start(out=dbg_xr2, in_=xres2),
                     "sSt", 16)
                P.op("sync", lambda e: e.dma_start(out=dbg_xr1, in_=xres1),
                     "sSt", 16)
                P.op("sync", lambda e: e.dma_start(
                    out=dbg_wsl, in_=hb["wout_in"]), "sSt", 16)
                for k in range(4):
                    P.op("sync", lambda e, k=k: e.dma_start(
                        out=dbg_ot4[:, ts(k, 2048)], in_=ot_in[k]), "sSt", 16)
            P.fence()
            plan_ffn(xres2, out_ext, None, [])
        elif stage >= 3:
            P.fence()
            plan_attention()
            dbg_ot = dbg_tensor("ot", [ESL, SEQ], BF16)
            if dbg_ot is not None:
                P.wait("sync", "sSt", P.cnt("sSt"))
                for k in range(4):
                    P.op("sync", lambda e, k=k: e.dma_start(
                        out=dbg_ot[:, ts(k, 2048)],
                        in_=ot_in[k]), "sSt", 16)
        if stage == 2:
            dbg_qh = dbg_tensor("qh", [SEQ, ESL], BF16)
            dbg_kh = dbg_tensor("kh", [SEQ, ESL], BF16)
            dbg_vh = dbg_tensor("vh", [SEQ, ESL], BF16)
            if dbg_qh is not None:
                P.wait("sync", "sSt", P.cnt("sSt"))
                P.wait("sync", "sGs", P.cnt("sGs"))
                P.op("sync", lambda e: e.dma_start(out=dbg_qh, in_=qhat),
                     "sSt", 16)
                P.op("sync", lambda e: e.dma_start(out=dbg_kh, in_=khat),
                     "sSt", 16)
                P.op("sync", lambda e: e.dma_start(out=dbg_vh, in_=vhat),
                     "sSt", 16)

        P.wait("sync", "sSt", P.cnt("sSt"))
        P.wait("sync", "sWr", P.cnt("sWr"))
        P.wait("gpsimd", "sCC", P.cnt("sCC"))
        P.wait("gpsimd", "sGs", P.cnt("sGs"))

        block = ctx.enter_context(nc.Block())

        @block.sync
        def _(e):
            for fn in P.E["sync"]:
                fn(e)

        @block.tensor
        def _(e):
            for fn in P.E["tensor"]:
                fn(e)

        @block.vector
        def _(e):
            for fn in P.E["vector"]:
                fn(e)

        @block.scalar
        def _(e):
            for fn in P.E["scalar"]:
                fn(e)

        @block.gpsimd
        def _(e):
            for fn in P.E["gpsimd"]:
                fn(e)

    return nc, dbg_aps


def _alibi_slopes():
    m = np.arange(1, 33, dtype=np.float32) * (8.0 / 32)
    return (1.0 / np.power(2.0, m)).astype(np.float32)


def make_in_maps(inputs):
    hs = np.asarray(inputs["L0_attn_hidden_states"])
    hs = hs.reshape(SEQ, DM)
    slopes = _alibi_slopes()
    ident = np.eye(128, dtype=ml_dtypes.bfloat16)
    identf = np.eye(128, dtype=np.float32)
    dabs_x = np.repeat(np.asarray(inputs["L0_ffn_d_absmax"]), 64, axis=1)
    A = lambda k: np.asarray(inputs[k])
    maps = []
    for c in range(NCORES):
        maps.append({
            "x": np.ascontiguousarray(hs[ts(c, SLOC), :]),
            "gcodes": np.ascontiguousarray(A("L0_ffn_g_codes")[:, ts(c, 512)]),
            "gabs": np.ascontiguousarray(A("L0_ffn_g_absmax")[:, ts(c, 8)]),
            "ucodes": np.ascontiguousarray(A("L0_ffn_u_codes")[:, ts(c, 512)]),
            "uabs": np.ascontiguousarray(A("L0_ffn_u_absmax")[:, ts(c, 8)]),
            "dcodes": np.ascontiguousarray(A("L0_ffn_d_codes")[:, ts(c, 1376)]),
            "dabsx": np.ascontiguousarray(dabs_x[:, ts(c, 1376)]),
            "glA": A("L0_ffn_g_lora_a"), "glB": A("L0_ffn_g_lora_b"),
            "ulA": A("L0_ffn_u_lora_a"), "ulB": A("L0_ffn_u_lora_b"),
            "dlA": A("L0_ffn_d_lora_a"), "dlB": A("L0_ffn_d_lora_b"),
            "wqkv_sl": np.ascontiguousarray(np.concatenate(
                [A("L0_attn_Wqkv")[s * DM + 512 * c: s * DM + 512 * c + 512]
                 for s in range(3)], axis=0)),
            "wout_sl": np.ascontiguousarray(A("L0_attn_Wout")[:, ts(c, 512)]),
            "qlnw": np.ascontiguousarray(A("L0_attn_q_ln_w")[ts(c, 512)]),
            "qlnb": np.ascontiguousarray(A("L0_attn_q_ln_b")[ts(c, 512)]),
            "klnw": np.ascontiguousarray(A("L0_attn_k_ln_w")[ts(c, 512)]),
            "klnb": np.ascontiguousarray(A("L0_attn_k_ln_b")[ts(c, 512)]),
            "slopes": np.ascontiguousarray(slopes[None, ts(c, 4)]),
            "ident": ident, "identf": identf,
        })
    return maps


def kernel(**inputs):
    nc, _ = build_program(stage=4)
    maps = make_in_maps(inputs)
    res = run_bass_kernel_spmd(nc, maps, core_ids=list(range(NCORES)))
    out = np.concatenate([np.asarray(res.results[c]["out"])
                          for c in range(NCORES)], axis=0)
    return out[None].astype(np.float32)
